# version 05
# baseline (speedup 1.0000x reference)
import bisect
import numpy as np
import ml_dtypes
import concourse.bass as bass
import concourse.mybir as mybir
from concourse.bass_utils import run_bass_kernel_spmd

F32 = mybir.dt.float32
BF16 = mybir.dt.bfloat16
AF = mybir.ActivationFunctionType
ALU = mybir.AluOpType
AX = mybir.AxisListType
BF = ml_dtypes.bfloat16

SEQ = 2
T = 2048
D = 1024
NCH = 8
EPS_LN = 1e-5

N_SUB = 4
DBG = False


def _rope_tables_ffn():
    d = np.arange(0, 64, 2, dtype=np.float64)
    inv = 1.0 / 10000.0 ** (d / 64.0)
    t = np.arange(T, dtype=np.float64)
    fr = t[None, :] * inv[:, None]
    fr = np.concatenate([fr, fr], 0)
    base = (d + 0.4 * 64.0) / (1.4 * 64.0)
    power = (t - T // 2) / (256 // 2)
    sc = base[:, None] ** power[None, :]
    sc = np.concatenate([sc, sc], 0)
    def dup(a):
        return np.ascontiguousarray(np.concatenate([a, a], 0)).astype(BF)
    return (dup(np.cos(fr) * sc), dup(np.sin(fr) * sc),
            dup(np.cos(fr) / sc), dup(np.sin(fr) / sc))

def _rope_tables_b():
    half = np.arange(32, dtype=np.float64)
    inv = 1.0 / 10000.0 ** (half / 32.0)
    t = np.arange(T, dtype=np.float64)
    fr = t[None, :] * inv[:, None]
    cos = np.repeat(np.cos(fr), 2, axis=0)
    sin = np.repeat(np.sin(fr), 2, axis=0)
    def dup(a):
        return np.ascontiguousarray(np.concatenate([a, a], 0)).astype(BF)
    return dup(cos), dup(sin)

def _rot_interleave_T():
    R = np.zeros((128, 128), np.float32)
    for b in (0, 64):
        for i in range(32):
            R[b + 2 * i, b + 2 * i + 1] = -1.0
            R[b + 2 * i + 1, b + 2 * i] = 1.0
    return np.ascontiguousarray(R.T).astype(BF)

def _rot_half_T():
    R = np.zeros((128, 128), np.float32)
    for b in (0, 64):
        for j in range(32):
            R[b + j, b + j + 32] = -1.0
            R[b + j + 32, b + j] = 1.0
    return np.ascontiguousarray(R.T).astype(BF)

def _local_masks():
    NEG = np.float32(-1e30)
    i = np.arange(128)[:, None]
    j = np.arange(384)[None, :]
    base = np.where((j >= i) & (j <= i + 256), np.float32(0), NEG)
    w0p0 = np.where((j >= np.maximum(i, 256)) & (j <= i + 256), np.float32(0), NEG)
    w0p1 = np.where((j >= np.maximum(i, 128)) & (j <= i + 256), np.float32(0), NEG)
    return base.astype(BF), w0p0.astype(BF), w0p1.astype(BF)

def _tri_mask():
    i = np.arange(128)[:, None]
    j = np.arange(128)[None, :]
    return np.where(j <= i, np.float32(0), np.float32(-1e30)).astype(BF)

def _ind16(c):
    a = np.zeros((128, 16), np.float32)
    a[0:64, 2 * c] = 1.0
    a[64:128, 2 * c + 1] = 1.0
    return a.astype(BF)

def _sel16(c):
    a = np.zeros((16, 128), np.float32)
    a[2 * c, 0:64] = 1.0
    a[2 * c + 1, 64:128] = 1.0
    return a.astype(BF)


class Ev:
    __slots__ = ("eng", "idx", "epoch", "hz")
    def __init__(self, eng, idx, epoch, hz=False):
        self.eng, self.idx, self.epoch = eng, idx, epoch
        self.hz = hz

class Eng:
    def __init__(self, mk, h, name):
        self.mk, self.h, self.name = mk, h, name
        self.sem = mk.nc.alloc_semaphore("es_" + name)
        self.insts = []
        self.cidx = []
        self.cval = []
        self.nval = 0
        self.waits = {}
    def emit(self, inst, hz=False):
        self.insts.append(inst)
        return Ev(self, len(self.insts) - 1, self.mk.epoch, hz)
    def cover(self, idx):
        pos = bisect.bisect_left(self.cidx, idx)
        if pos < len(self.cidx):
            return self.cval[pos]
        self.nval += 1
        self.insts[idx].then_inc(self.sem, 1)
        self.cidx.append(idx)
        self.cval.append(self.nval)
        return self.nval

class DSem:
    __slots__ = ("sem", "n", "name")
    def __init__(self, mk, name):
        self.sem = mk.nc.alloc_semaphore("ds_" + name)
        self.n = 0
        self.name = name
    def attach(self, inst):
        inst.then_inc(self.sem, 16)
        self.n += 16
        return self.n

class Buf:
    __slots__ = ("w", "rs")
    def __init__(self):
        self.w = None
        self.rs = []

class MK:
    def __init__(self):
        nc = bass.Bass("TRN2", target_bir_lowering=False, disable_frame_to_traceback=True)
        self.nc = nc
        self.epoch = 0
        self.pe = Eng(self, nc.tensor, "pe")
        self.dv = Eng(self, nc.vector, "dv")
        self.sc = Eng(self, nc.scalar, "sc")
        self.qs = Eng(self, nc.sync, "qs")
        self.qg = Eng(self, nc.gpsimd, "qg")
        self._dsems = {}
        self._dram_w = {}
        self._dram_r = {}
        self._store_dsems = set()
        self._claims = []
        self._arena = None
        self._arena_off = 0
        self._consts = []

    def init_arena(self):
        free = self.nc.sbuf_top - self.nc.sbuf_base - 64
        self.arena_size = free
        self._arena = self.nc.alloc_sbuf_tensor("arena", [128, free], mybir.dt.uint8)
        self.arena_addr = self.nc.lookup_mloc(self._arena).addr

    def claim(self, off, size, buf):
        evs = []
        keep = []
        for c in self._claims:
            if c[0] < off + size and off < c[1]:
                ob = c[2]
                if ob is not buf:
                    evs.extend(ob.rs)
                    if ob.w is not None:
                        evs.append(ob.w)
                if c[0] < off:
                    keep.append([c[0], off, c[2]])
                if c[1] > off + size:
                    keep.append([off + size, c[1], c[2]])
            else:
                keep.append(c)
        keep.append([off, off + size, buf])
        self._claims = keep
        return evs

    def place(self, name, shape, dtype, off):
        per_part = int(np.prod(shape[1:])) * mybir.dt.size(dtype)
        assert off % 32 == 0, (name, off)
        assert off + per_part <= self.arena_size, (name, off, per_part)
        h = self.nc.virtual_sbuf_tensor(name, list(shape), dtype)
        self.nc._tensor(h.name, list(h.shape), h.dtype, type="SB")
        mloc = self.nc.lookup_mloc(h)
        mloc.allocated = True
        mloc.addr = self.arena_addr + off
        return h

    def bump(self, name, shape, dtype):
        off = (self._arena_off + 31) & ~31
        h = self.place(name, shape, dtype, off)
        self._arena_off = off + int(np.prod(shape[1:])) * mybir.dt.size(dtype)
        return h

    def const(self, name, arr):
        d = self.nc.inline_tensor(np.ascontiguousarray(arr), name="c_" + name)
        dt = BF16 if arr.dtype == BF else F32
        h = self.bump("sb_" + name, list(arr.shape), dt)
        self._consts.append((h, d))
        return h

    def const_dram(self, name, arr):
        return self.nc.inline_tensor(np.ascontiguousarray(arr), name="c_" + name)

    def dsem(self, name):
        if name not in self._dsems:
            self._dsems[name] = DSem(self, name)
        return self._dsems[name]

    def dep(self, eng, ev, war=False):
        if ev is None:
            return
        if isinstance(ev, tuple):
            _, ds, val = ev
            val = ds.n
            if eng.waits.get(ds, 0) >= val:
                return
            eng.h.wait_ge(ds.sem, val)
            eng.waits[ds] = val
            return
        if ev.eng is eng:
            if not ev.hz:
                return
            val = ev.eng.cover(ev.idx)
            if eng.waits.get(eng, 0) >= val:
                return
            eng.h.wait_ge(eng.sem, val)
            eng.waits[eng] = val
            return
        if ev.epoch < self.epoch:
            return
        val = ev.eng.cover(ev.idx)
        if eng.waits.get(ev.eng, 0) >= val:
            return
        eng.h.wait_ge(ev.eng.sem, val)
        eng.waits[ev.eng] = val

    def rd(self, eng, bufs):
        for b in bufs:
            self.dep(eng, b.w)

    def wr(self, eng, bufs):
        for b in bufs:
            for r in b.rs:
                self.dep(eng, r, war=True)
            self.dep(eng, b.w, war=True)

    def op(self, eng, fn, reads=(), writes=(), hz=False):
        self.rd(eng, reads)
        self.wr(eng, writes)
        inst = fn()
        ev = eng.emit(inst, hz or eng is self.dv or eng is self.sc)
        for b in reads:
            b.rs.append(ev)
        for b in writes:
            b.w = ev
            b.rs = []
        return ev

    def dma(self, q, dsname, dst_ap, dst_bufs, src_ap, src_bufs,
            src_key=None, dst_key=None):
        ds = self.dsem(dsname)
        if src_key is not None:
            for ds2, val2 in self._dram_w.get(src_key, {}).items():
                self.dep(q, ('d', ds2, val2))
        if dst_key is not None:
            for ds2, val2 in self._dram_r.get(dst_key, {}).items():
                self.dep(q, ('d', ds2, val2))
            for ds2, val2 in self._dram_w.get(dst_key, {}).items():
                self.dep(q, ('d', ds2, val2))
        self.rd(q, src_bufs)
        self.wr(q, dst_bufs)
        from concourse.bass_primitives import MemorySpace as _MS
        if dst_ap.space != _MS.SBUF:
            self._store_dsems.add(ds)
        inst = q.h.dma_start(out=dst_ap, in_=src_ap)
        val = ds.attach(inst)
        ev = ('d', ds, val)
        if src_key is not None:
            self._dram_r.setdefault(src_key, {})[ds] = val
        if dst_key is not None:
            self._dram_w.setdefault(dst_key, {})[ds] = val
        for b in dst_bufs:
            b.w = ev
            b.rs = []
        for b in src_bufs:
            b.rs.append(ev)
        return ev

    def barrier(self):
        for ds in self._store_dsems:
            for eng in (self.dv, self.sc, self.qs, self.qg):
                if eng.waits.get(ds, 0) < ds.n:
                    eng.h.wait_ge(ds.sem, ds.n)
                    eng.waits[ds] = ds.n
        self.nc.all_engine_barrier()
        self.epoch += 1


class Model:
    def __init__(self):
        m = MK()
        self.m = m
        nc = m.nc
        self.x_in = nc.dram_tensor("x_fm", [SEQ, NCH, 128, T], F32, kind="ExternalInput")
        self.out = nc.dram_tensor("out_fm", [SEQ, NCH, 128, T], F32, kind="ExternalOutput")
        self.hdram = nc.dram_tensor("hdram", [SEQ, NCH, 128, T], F32, kind="Internal")
        self.hinner = nc.dram_tensor("hinner", [NCH, 128, T], F32, kind="Internal")
        self.waq = nc.dram_tensor("waq", [D, D], F32, kind="ExternalInput")
        self.wak = nc.dram_tensor("wak", [D, D], F32, kind="ExternalInput")
        self.wav = nc.dram_tensor("wav", [D, D], F32, kind="ExternalInput")
        self.wbqkv = nc.dram_tensor("wbqkv", [D, 3 * D], F32, kind="ExternalInput")
        self.wbo = nc.dram_tensor("wbo", [D, D], F32, kind="ExternalInput")
        self.bbo = nc.dram_tensor("bbo", [128, NCH], F32, kind="ExternalInput")
        self.fqkv = nc.dram_tensor("fqkv", [6, D, 3 * D], F32, kind="ExternalInput")
        self.fwo = nc.dram_tensor("fwo", [6, D, D], F32, kind="ExternalInput")
        self.fw1 = nc.dram_tensor("fw1", [6, D, 5460], F32, kind="ExternalInput")
        self.fb1a = nc.dram_tensor("fb1a", [6, 128, 22], F32, kind="ExternalInput")
        self.fb1g = nc.dram_tensor("fb1g", [6, 128, 22], F32, kind="ExternalInput")
        self.fw2 = nc.dram_tensor("fw2", [6, 2730, D], F32, kind="ExternalInput")
        self.fb2 = nc.dram_tensor("fb2", [6, 128, NCH], F32, kind="ExternalInput")
        self.fg = nc.dram_tensor("fg", [6, 128, NCH], F32, kind="ExternalInput")
        self.fbeta = nc.dram_tensor("fbeta", [6, 128, NCH], F32, kind="ExternalInput")
        self.fqs = nc.dram_tensor("fqs", [6, 128, 1], F32, kind="ExternalInput")
        self.fks = nc.dram_tensor("fks", [6, 128, 1], F32, kind="ExternalInput")
        if DBG:
            self.dbg = nc.dram_tensor("dbg", [8, NCH, 128, T], F32, kind="ExternalOutput")
            self.b_dbg = Buf()
        self.cur_seq = 0
        self.dbg_i = 0

        m.init_arena()
        self._build_consts()
        self.psA = m.nc.alloc_psum_tensor("psA", [128, 2048], F32)
        self.psB = m.nc.alloc_psum_tensor("psB", [128, 2048], F32)
        self.psA_bf = self.psA.bitcast(BF16)
        self.psB_bf = self.psB.bitcast(BF16)
        self.bA = [Buf() for _ in range(4)]
        self.bB = [Buf() for _ in range(4)]
        self.dyn_base = (m._arena_off + 31) & ~31
        self._bufs = {}
        self.bias1_done = [False] * 6
        self.fix_cur = 0

    def _build_consts(self):
        m = self.m
        self.c_id = m.const("ident", np.eye(128, dtype=BF))
        self.c_idf32 = m.const("identf", np.eye(128, dtype=np.float32))
        self.c_ones = m.const("ones", np.ones((128, 1), np.float32).astype(BF))
        self.c_brow = m.const("brow", np.ones((1, 128), np.float32))
        self.c_ind16 = [m.const(f"ind16_{c}", _ind16(c)) for c in range(8)]
        self.c_sel16 = [m.const(f"sel16_{c}", _sel16(c)) for c in range(8)]
        mb, m0, m1 = _local_masks()
        self.c_maskL = m.const("maskL", mb)
        self.c_maskL0 = [m.const("maskL0p0", m0), m.const("maskL0p1", m1)]
        self.c_tri = m.const("tri", _tri_mask())
        self.c_rotI = m.const("rotI", _rot_interleave_T())
        self.c_rotH = m.const("rotH", _rot_half_T())
        cs, ss, ci, si = _rope_tables_ffn()
        self.d_cs = m.const_dram("tab_cs", cs)
        self.d_ss = m.const_dram("tab_ss", ss)
        self.d_ci = m.const_dram("tab_ci", ci)
        self.d_si = m.const_dram("tab_si", si)
        bc, bs = _rope_tables_b()
        self.d_bcos = m.const_dram("tab_bcos", bc)
        self.d_bsin = m.const_dram("tab_bsin", bs)
        self.bias1a = m.bump("bias1a", [128, 6, 22], F32)
        self.bias1g = m.bump("bias1g", [128, 6, 22], F32)
        self.b_bias1 = Buf()
        self.vparam = m.bump("vparam", [128, 16], F32)
        self.b_vparam = Buf()
        self.pcol = m.bump("pcol", [128, 48], F32)
        self.b_pcol = Buf()
        self.rn16b = m.bump("rn16b", [16, 512], BF16)
        self.b_rn16b = Buf()
        self.pcolb = m.bump("pcolb", [128, 8], BF16)
        self.b_pcolb = Buf()
        self.rows = m.bump("rows", [128, 1536], F32)
        self.b_rows = Buf()

    def region(self, name, shape, dtype, cur):
        off = (cur[0] + 31) & ~31
        sz = int(np.prod(shape[1:])) * mybir.dt.size(dtype)
        h = self.m.place(name, shape, dtype, self.dyn_base + off)
        cur[0] = off + sz
        b = Buf()
        b.rs = list(self.m.claim(self.dyn_base + off, sz, b))
        self._bufs[h.name] = b
        return h

    def region_alias(self, name, shape, dtype, other):
        off = self.m.nc.lookup_mloc(other).addr - self.m.arena_addr
        h = self.m.place(name, shape, dtype, off)
        self._bufs[h.name] = self._bufs[other.name]
        return h

    def B(self, h):
        return self._bufs[h.name]

    def load_consts(self):
        m = self.m
        for h, d in m._consts:
            m.dma(m.qs, "consts", h[:], [], d.ap(), [])
        ds = m.dsem("consts")
        for e in (m.pe, m.dv, m.sc):
            e.h.wait_ge(ds.sem, ds.n)
            e.waits[ds] = ds.n
        m.barrier()

    def ln_stats(self, xf, sqf, b_src, W, ps0, b0, ps1, b1, pm, bm, pr, br):
        m = self.m
        def sums(ps, f):
            last = None
            for c in range(NCH):
                last = m.nc.tensor.matmul(ps[0:1, 0:W], self.c_ones[:], f(c),
                                          start=(c == 0), stop=(c == NCH - 1))
            return last
        m.op(m.pe, lambda: sums(ps0, xf), reads=b_src, writes=[b0])
        m.op(m.pe, lambda: sums(ps1, sqf), reads=b_src, writes=[b1])
        rows, b_rows = self.rows, self.b_rows
        mu = rows[0:1, 0:W]
        rs = rows[0:1, 512:512 + W]
        v2 = rows[0:1, 1024:1024 + W]
        m.op(m.sc, lambda: m.nc.scalar.activation(mu, ps0[0:1, 0:W], AF.Copy, scale=1.0 / D),
             reads=[b0], writes=[b_rows])
        m.op(m.dv, lambda: m.nc.vector.tensor_scalar(v2, ps1[0:1, 0:W], 1.0 / D, float(EPS_LN),
                                                     ALU.mult, ALU.add),
             reads=[b1], writes=[b_rows])
        m.op(m.dv, lambda: m.nc.vector.scalar_tensor_tensor(rs, mu, -1.0, mu, ALU.mult, ALU.mult),
             reads=[b_rows], writes=[b_rows])
        m.op(m.dv, lambda: m.nc.vector.tensor_add(v2, v2, rs),
             reads=[b_rows], writes=[b_rows])
        m.op(m.sc, lambda: m.nc.scalar.activation(rs, v2, AF.Ln),
             reads=[b_rows], writes=[b_rows])
        m.op(m.sc, lambda: m.nc.scalar.activation(rs, rs, AF.Exp, scale=-0.5),
             reads=[b_rows], writes=[b_rows])
        m.op(m.pe, lambda: m.nc.tensor.matmul(pm[:, 0:W], self.c_brow[:], mu,
                                              start=True, stop=True),
             reads=[b_rows], writes=[bm])
        m.op(m.pe, lambda: m.nc.tensor.matmul(pr[:, 0:W], self.c_brow[:], rs,
                                              start=True, stop=True),
             reads=[b_rows], writes=[br])

    def attn_a(self, src, dst, seq, src_key, dst_key):
        m = self.m
        cur = [0]
        xn = self.region("a_xn", [128, NCH, T], BF16, cur)
        K = self.region("a_K", [128, NCH, T], BF16, cur)
        V = self.region("a_V", [128, 16, 1024], BF16, cur)
        w0 = self.region("a_w0", [128, NCH, 1024], BF16, cur)
        w1 = self.region("a_w1", [128, NCH, 1024], BF16, cur)
        qg = self.region("a_qg", [128, NCH, 512], BF16, cur)
        P = self.region("a_P", [128, T], BF16, cur)
        PT = self.region("a_PT", [128, 16, 512], BF16, cur)
        xb = [self.region(f"a_xb{i}", [128, NCH, 512], BF16, cur) for i in range(2)]
        xst = self.region("a_xst", [128, NCH, 512], F32, cur)
        sq = self.region_alias("a_sq", [128, NCH, 512], BF16, xst)
        so = [self.region(f"a_so{i}", [128, 512], F32, cur) for i in range(2)]
        t1 = self.region_alias("a_t1", [128, 512], F32, so[0])
        b_xn, b_K, b_V = self.B(xn), self.B(K), self.B(V)
        b_w0, b_w1, b_qg = self.B(w0), self.B(w1), self.B(qg)
        b_P, b_PT, b_xst = self.B(P), self.B(PT), self.B(xst)
        b_sq = b_xst
        b_xb = [self.B(x) for x in xb]
        b_so = [self.B(x) for x in so]
        b_t1 = b_so[0]

        srcv = src.rearrange("c p t -> p c t")
        dstv = dst.rearrange("c p t -> p c t")
        tag = f"aa{seq}"

        m.dma(m.qg, tag + "w0", w0[:], [b_w0], self.wak.rearrange("(c p) o -> p c o", p=128), [])
        m.dma(m.qg, tag + "w1", w1[:], [b_w1], self.wav.rearrange("(c p) o -> p c o", p=128), [])

        for tb in range(4):
            cols = slice(tb * 512, tb * 512 + 512)
            par = tb % 2
            m.dma(m.qg, tag + f"xb{par}", xb[par][:], [b_xb[par]], srcv[:, :, cols], [],
                  src_key=src_key)
            m.op(m.sc, lambda par=par: m.nc.scalar.activation(sq[:], xb[par][:], AF.Square),
                 reads=[b_xb[par]], writes=[b_sq])
            self.ln_stats(lambda c, par=par: xb[par][:, c, :], lambda c: sq[:, c, :],
                          [b_xb[par], b_sq], 512,
                          self.psA[:, 0:512], self.bA[0], self.psA[:, 512:1024], self.bA[1],
                          self.psA[:, 1024:1536], self.bA[2], self.psA[:, 1536:2048], self.bA[3])
            for c in range(NCH):
                m.op(m.dv, lambda c=c, par=par: m.nc.vector.tensor_sub(
                    t1[:], xb[par][:, c, :], self.psA[:, 1024:1536]),
                     reads=[b_xb[par], self.bA[2]], writes=[b_t1])
                m.op(m.dv, lambda c=c, cols=cols: m.nc.vector.tensor_mul(
                    xn[:, c, cols], t1[:], self.psA[:, 1536:2048]),
                     reads=[b_t1, self.bA[3]], writes=[b_xn])
            for oc in range(NCH):
                pk = self.psB[:, (oc % 2) * 512:(oc % 2) * 512 + 512]
                bk = self.bB[oc % 2]
                def kmms(oc=oc, pk=pk, cols=cols):
                    last = None
                    for e in range(NCH):
                        last = m.nc.tensor.matmul(pk, w0[:, e, oc * 128:oc * 128 + 128],
                                                  xn[:, e, cols], start=(e == 0), stop=(e == NCH - 1))
                    return last
                m.op(m.pe, kmms, reads=[b_w0, b_xn], writes=[bk])
                m.op(m.sc, lambda oc=oc, pk=pk, cols=cols: m.nc.scalar.copy(K[:, oc, cols], pk),
                     reads=[bk], writes=[b_K])
            for i in range(4):
                tt = 4 * tb + i
                for oh in range(2):
                    r = (i * 2 + oh) % 2
                    pv = self.psB[:, (2 + r) * 512:(2 + r) * 512 + 512]
                    bv = self.bB[2 + r]
                    def vmms(tt=tt, oh=oh, pv=pv):
                        last = None
                        for e in range(NCH):
                            last = m.nc.tensor.matmul(
                                pv, xn[:, e, tt * 128:tt * 128 + 128],
                                w1[:, e, oh * 512:oh * 512 + 512],
                                start=(e == 0), stop=(e == NCH - 1))
                        return last
                    m.op(m.pe, vmms, reads=[b_w1, b_xn], writes=[bv])
                    m.op(m.sc, lambda tt=tt, oh=oh, pv=pv: m.nc.scalar.copy(
                        V[:, tt, oh * 512:oh * 512 + 512], pv),
                         reads=[bv], writes=[b_V])
        m.dma(m.qg, tag + "w0", w0[:], [b_w0], self.waq.rearrange("(c p) o -> p c o", p=128), [])

        for g in range(4):
            gc = slice(g * 512, g * 512 + 512)
            m.dma(m.qs, tag + "xst", xst[:], [b_xst], srcv[:, :, gc], [], src_key=src_key)
            for oc in range(NCH):
                pq = self.psB[:, (oc % 2) * 512:(oc % 2) * 512 + 512]
                bq = self.bB[oc % 2]
                def qmms(oc=oc, pq=pq, gc=gc):
                    last = None
                    for e in range(NCH):
                        last = m.nc.tensor.matmul(pq, w0[:, e, oc * 128:oc * 128 + 128],
                                                  xn[:, e, gc], start=(e == 0), stop=(e == NCH - 1))
                    return last
                m.op(m.pe, qmms, reads=[b_w0, b_xn], writes=[bq])
                m.op(m.sc, lambda oc=oc, pq=pq: m.nc.scalar.activation(
                    qg[:, oc, :], pq, AF.Copy, scale=1.0 / 32.0),
                     reads=[bq], writes=[b_qg])
            for qt in range(4):
                qq = slice(qt * 128, qt * 128 + 128)
                bS = self.bA[0]
                def smms(qq=qq):
                    last = None
                    for kb in range(4):
                        for e in range(NCH):
                            last = m.nc.tensor.matmul(
                                self.psA[:, kb * 512:kb * 512 + 512],
                                qg[:, e, qq], K[:, e, kb * 512:kb * 512 + 512],
                                start=(e == 0), stop=(e == NCH - 1))
                    return last
                m.wr(m.pe, [self.bA[1], self.bA[2], self.bA[3]])
                m.op(m.pe, smms, reads=[b_qg, b_K], writes=[bS])
                for bx in (self.bA[1], self.bA[2], self.bA[3]):
                    bx.w = bS.w
                    bx.rs = []
                m.op(m.dv, lambda: m.nc.vector.reduce_max(self.vparam[:, 1:2], self.psA[:],
                                                           axis=AX.X, negate=True),
                     reads=[bS], writes=[self.b_vparam], hz=True)
                m.op(m.sc, lambda: m.nc.scalar.activation(
                    P[:], self.psA[:], AF.Exp, bias=self.vparam[:, 1:2],
                    accum_out=self.vparam[:, 2:3]),
                     reads=[bS, self.b_vparam], writes=[b_P, self.b_vparam])
                m.op(m.dv, lambda: m.nc.vector.reciprocal(self.vparam[:, 3:4], self.vparam[:, 2:3]),
                     reads=[self.b_vparam], writes=[self.b_vparam], hz=True)
                m.op(m.dv, lambda: m.nc.vector.tensor_scalar_mul(P[:], P[:], self.vparam[:, 3:4]),
                     reads=[self.b_vparam], writes=[b_P])
                for half in range(2):
                    pb0 = (2 + half) * 1024
                    bb = self.bB[2 + half]
                    def trs(pb0=pb0, half=half):
                        last = None
                        for k in range(8):
                            kt = half * 8 + k
                            last = m.nc.tensor.transpose(
                                self.psB_bf[:, pb0 + k * 128:pb0 + k * 128 + 128],
                                P[:, kt * 128:kt * 128 + 128], self.c_id[:])
                        return last
                    m.op(m.pe, trs, reads=[b_P], writes=[bb])
                    m.op(m.sc, lambda pb0=pb0, half=half, qq=qq: m.nc.scalar.copy(
                        PT[:, half * 8:half * 8 + 8, qq],
                        self.psB_bf[:, pb0:pb0 + 1024].rearrange("p (k q) -> p k q", k=8)),
                         reads=[bb], writes=[b_PT])
            for e in range(NCH):
                pp = self.psB[:, (e % 2) * 512:(e % 2) * 512 + 512]
                bp = self.bB[e % 2]
                def pvmms(e=e, pp=pp):
                    last = None
                    for kt in range(16):
                        last = m.nc.tensor.matmul(pp, V[:, kt, e * 128:e * 128 + 128],
                                                  PT[:, kt, :], start=(kt == 0), stop=(kt == 15))
                    return last
                m.op(m.pe, pvmms, reads=[b_V, b_PT], writes=[bp])
                sob = so[e % 2]
                bsob = b_so[e % 2]
                m.op(m.dv, lambda e=e, pp=pp, sob=sob: m.nc.vector.tensor_add(
                    sob[:], pp, xst[:, e, :]),
                     reads=[bp, b_xst], writes=[bsob])
                m.dma(m.qs, tag + "st", dstv[:, e, gc], [], sob[:], [bsob],
                      dst_key=dst_key)

    TAPS = {0: 0, 1: 1, 2: 2, 3: 3, 4: 4, 5: 5, 6: 6, 7: 7}

    TAP_SEQ = 0

    def tap_hinner(self):
        if not DBG or self.cur_seq != self.TAP_SEQ:
            return
        o = self.dbg_i
        self.dbg_i += 1
        if o not in self.TAPS:
            return
        m = self.m
        m.dma(m.qs, "dbgtap", self.dbg[self.TAPS[o]], [], self.hinner.ap(), [],
              src_key="hin")

    def ffn_block(self, src, dst, seq, src_key, dst_key):
        m = self.m
        cur = [0]
        tabs = self.region("f_tabs", [128, 4, T], BF16, cur)
        wo = self.region("f_wo", [128, NCH, 1024], BF16, cur)
        self.fix_cur = cur[0]
        b_tabs, b_wo = self.B(tabs), self.B(wo)

        srcv = src.rearrange("c p t -> p c t")
        dstv = dst.rearrange("c p t -> p c t")
        hv = self.hinner.rearrange("c p t -> p c t")

        m.dma(m.qs, "tabs", tabs[:, 0, :], [b_tabs], self.d_cs.ap(), [])
        m.dma(m.qs, "tabs", tabs[:, 1, :], [b_tabs], self.d_ss.ap(), [])
        m.dma(m.qs, "tabs", tabs[:, 2, :], [b_tabs], self.d_ci.ap(), [])
        m.dma(m.qs, "tabs", tabs[:, 3, :], [b_tabs], self.d_si.ap(), [])

        self.ffn_open(srcv, hv, src_key)
        self.tap_hinner()
        for i in range(6):
            self.inner_attn(i, wo, b_wo, tabs, b_tabs)
            self.tap_hinner()
            self.inner_ffn(i)
            self.tap_hinner()
        self.ffn_close(srcv, hv, dstv, src_key, dst_key)

    def ffn_open(self, srcv, hv, src_key):
        m = self.m
        cur = [self.fix_cur]
        xb = [self.region(f"fo_xb{i}", [128, NCH, 512], BF16, cur) for i in range(2)]
        xf = [self.region(f"fo_xf{i}", [128, NCH, 512], F32, cur) for i in range(2)]
        t1 = self.region("fo_t1", [128, 512], F32, cur)
        ho = [self.region(f"fo_ho{i}", [128, NCH, 512], F32, cur) for i in range(2)]
        b_xb = [self.B(x) for x in xb]
        b_xf = [self.B(x) for x in xf]
        b_t1 = self.B(t1)
        b_ho = [self.B(x) for x in ho]
        sq = self.region_alias("fo_sq", [128, NCH, 512], BF16, ho[0])
        b_sq = b_ho[0]
        for tb in range(4):
            cols = slice(tb * 512, tb * 512 + 512)
            par = tb % 2
            m.dma(m.qg, f"fo_xb{par}", xb[par][:], [b_xb[par]], srcv[:, :, cols], [],
                  src_key=src_key)
            m.dma(m.qs, f"fo_xf{par}", xf[par][:], [b_xf[par]], srcv[:, :, cols], [],
                  src_key=src_key)
            m.op(m.sc, lambda par=par: m.nc.scalar.activation(sq[:], xb[par][:], AF.Square),
                 reads=[b_xb[par]], writes=[b_sq])
            self.ln_stats(lambda c, par=par: xb[par][:, c, :], lambda c: sq[:, c, :],
                          [b_xb[par], b_sq], 512,
                          self.psA[:, 0:512], self.bA[0], self.psA[:, 512:1024], self.bA[1],
                          self.psA[:, 1024:1536], self.bA[2], self.psA[:, 1536:2048], self.bA[3])
            for c in range(NCH):
                m.op(m.dv, lambda c=c, par=par: m.nc.vector.tensor_sub(
                    t1[:], xf[par][:, c, :], self.psA[:, 1024:1536]),
                     reads=[b_xf[par], self.bA[2]], writes=[b_t1])
                m.op(m.dv, lambda c=c, par=par: m.nc.vector.tensor_mul(
                    ho[par][:, c, :], t1[:], self.psA[:, 1536:2048]),
                     reads=[b_t1, self.bA[3]], writes=[b_ho[par]])
            m.dma(m.qs, "fo_st", hv[:, :, cols], [], ho[par][:], [b_ho[par]],
                  dst_key="hin")

    def ffn_close(self, srcv, hv, dstv, src_key, dst_key):
        m = self.m
        cur = [self.fix_cur]
        hf = [self.region(f"fc_hf{i}", [128, NCH, 512], F32, cur) for i in range(2)]
        xf = [self.region(f"fc_xf{i}", [128, NCH, 512], F32, cur) for i in range(2)]
        b_hf = [self.B(x) for x in hf]
        b_xf = [self.B(x) for x in xf]
        for tb in range(4):
            cols = slice(tb * 512, tb * 512 + 512)
            par = tb % 2
            m.dma(m.qs, f"fc_h{par}", hf[par][:], [b_hf[par]], hv[:, :, cols], [],
                  src_key="hin")
            m.dma(m.qg, f"fc_x{par}", xf[par][:], [b_xf[par]], srcv[:, :, cols], [],
                  src_key=src_key)
            m.op(m.dv, lambda par=par: m.nc.vector.tensor_add(
                hf[par][:], hf[par][:], xf[par][:]),
                 reads=[b_hf[par], b_xf[par]], writes=[b_hf[par]])
            m.dma(m.qs, "fc_st", dstv[:, :, cols], [], hf[par][:], [b_hf[par]],
                  dst_key=dst_key)

    def inner_attn(self, li, wo, b_wo, tabs, b_tabs):
        m = self.m
        cur = [self.fix_cur]
        wq = self.region("ia_wq", [128, NCH, 1024], BF16, cur)
        wk = self.region("ia_wk", [128, NCH, 1024], BF16, cur)
        wv = self.region("ia_wv", [128, NCH, 1024], BF16, cur)
        kr = self.region("ia_kr", [128, NCH, 6, 256], BF16, cur)
        vr = self.region("ia_vr", [128, 10, 1024], BF16, cur)
        qr = self.region("ia_qr", [128, NCH, 2, 512], BF16, cur)
        qn = self.region("ia_qn", [128, NCH, 512], BF16, cur)
        rnb = self.region("ia_rnb", [128, NCH, 512], BF16, cur)
        aout = self.region("ia_aout", [128, NCH, 512], BF16, cur)
        hb = [self.region(f"ia_hb{i}", [128, NCH, 520], BF16, cur) for i in range(2)]
        hres = self.region_alias("ia_hr0", [128, NCH, 512], F32, qn)
        so = [self.region(f"ia_so{i}", [128, 512], F32, cur) for i in range(2)]
        P = self.region("ia_P", [128, 384], BF16, cur)
        PT = self.region("ia_PT", [128, 3, 128], BF16, cur)
        b_wq, b_wk, b_wv = self.B(wq), self.B(wk), self.B(wv)
        b_kr_m, b_vr_m, b_qr_m = self.B(kr), self.B(vr), self.B(qr)
        def _sub(master, n):
            out = []
            for _ in range(n):
                b = Buf()
                b.rs = list(master.rs)
                out.append(b)
            return out
        b_krp = _sub(b_kr_m, 3)
        b_vrg = _sub(b_vr_m, 5)
        b_qrp = _sub(b_qr_m, 2)
        b_qn = self.B(qn)
        b_rnb = b_qn
        self._bufs[rnb.name] = b_qn
        self._bufs[hres.name] = b_qn
        b_aout = self.B(aout)
        b_P, b_PT = self.B(P), self.B(PT)
        b_hres = b_qn
        b_sq = b_rnb
        b_hb = [self.B(x) for x in hb]
        b_so = [self.B(x) for x in so]
        hv = self.hinner.rearrange("c p t -> p c t")
        wvv = self.fqkv[li].rearrange("(c p) o -> p c o", p=128)

        m.dma(m.qg, "ia_wq", wq[:], [b_wq], wvv[:, :, 0:1024], [])
        m.dma(m.qg, "ia_wk", wk[:], [b_wk], wvv[:, :, 1024:2048], [])
        m.dma(m.qg, "ia_wv", wv[:], [b_wv], wvv[:, :, 2048:3072], [])
        m.dma(m.qg, "ia_wo", wo[:], [b_wo],
              self.fwo[li].rearrange("(c p) o -> p c o", p=128), [])
        m.dma(m.qs, "pcol", self.pcol[:, 32:33], [self.b_pcol], self.fqs[li], [])
        m.dma(m.qs, "pcol", self.pcol[:, 33:34], [self.b_pcol], self.fks[li], [])
        m.op(m.dv, lambda: m.nc.vector.tensor_scalar_mul(
            self.pcol[:, 32:33], self.pcol[:, 32:33], 8.0),
             reads=[self.b_pcol], writes=[self.b_pcol])

        m.op(m.dv, lambda: m.nc.vector.memset(kr[:, :, 5, :], 0.0), writes=[b_krp[2]])
        m.op(m.dv, lambda: m.nc.vector.memset(vr[:, 8:10, :], 0.0), writes=[b_vrg[4]])

        hb_ev = {}

        def produce(sb):
            par = sb % 2
            t0 = 512 * sb
            if sb == 0:
                m.op(m.dv, lambda: m.nc.vector.memset(hb[0][:, :, 0:1], 0.0),
                     writes=[b_hb[0]])
                hb_ev[0] = m.dma(m.qg, f"ia_hb{par}", hb[0][:, :, 1:513], [b_hb[0]],
                                 hv[:, :, 0:512], [], src_key="hin")
            else:
                hb_ev[sb] = m.dma(m.qg, f"ia_hb{par}", hb[par][:, :, 0:513], [b_hb[par]],
                                  hv[:, :, t0 - 1:t0 + 512], [], src_key="hin")
            bhb = b_hb[par]
            def hrhs(c, lo, n, par=par):
                off = 1 if c < 4 else 0
                return hb[par][:, c, off + lo: off + lo + n]
            for i in range(4):
                for oh in range(2):
                    r = (i * 2 + oh) % 2
                    pb = self.psA[:, r * 512:r * 512 + 512]
                    bb = self.bA[r]
                    def vmms(i=i, oh=oh, pb=pb):
                        last = None
                        for e in range(NCH):
                            last = m.nc.tensor.matmul(
                                pb, hrhs(e, i * 128, 128),
                                wv[:, e, oh * 512:oh * 512 + 512],
                                start=(e == 0), stop=(e == NCH - 1))
                        return last
                    m.op(m.pe, vmms, reads=[bhb, b_wv], writes=[bb])
                    m.op(m.sc, lambda i=i, oh=oh, pb=pb, sb=sb: m.nc.scalar.copy(
                        vr[:, (4 * sb + i) % 10, oh * 512:oh * 512 + 512], pb),
                         reads=[bb], writes=[b_vrg[((4 * sb + i) % 10) // 2]])
            for which in range(2):
                wmat = wq if which == 0 else wk
                bwm = b_wq if which == 0 else b_wk
                for oc in range(NCH):
                    r = 2 + oc % 2
                    pb = self.psA[:, r * 512:r * 512 + 512]
                    bb = self.bA[r]
                    def pmms(oc=oc, pb=pb, wmat=wmat):
                        last = None
                        for e in range(NCH):
                            last = m.nc.tensor.matmul(
                                pb, wmat[:, e, oc * 128:oc * 128 + 128],
                                hrhs(e, 0, 512), start=(e == 0), stop=(e == NCH - 1))
                        return last
                    m.op(m.pe, pmms, reads=[bhb, bwm], writes=[bb])
                    m.op(m.sc, lambda oc=oc, pb=pb: m.nc.scalar.copy(qn[:, oc, :], pb),
                         reads=[bb], writes=[b_qn])
                m.op(m.sc, lambda: m.nc.scalar.activation(rnb[:], qn[:], AF.Square),
                     reads=[b_qn], writes=[b_sq])
                def ssum():
                    last = None
                    for c in range(NCH):
                        last = m.nc.tensor.matmul(self.psA[0:16, 0:512], self.c_ind16[c][:],
                                                  rnb[:, c, :], start=(c == 0), stop=(c == NCH - 1))
                    return last
                m.op(m.pe, ssum, reads=[b_sq], writes=[self.bA[0]])
                rows, b_rows = self.rows, self.b_rows
                r1 = rows[0:16, 0:512]
                r2 = rows[0:16, 512:1024]
                m.op(m.dv, lambda: m.nc.vector.tensor_scalar_max(r1, self.psA[0:16, 0:512], 1e-24),
                     reads=[self.bA[0]], writes=[b_rows])
                m.op(m.sc, lambda: m.nc.scalar.activation(r2, r1, AF.Ln),
                     reads=[b_rows], writes=[b_rows])
                m.op(m.sc, lambda: m.nc.scalar.activation(r1, r2, AF.Exp, scale=-0.5),
                     reads=[b_rows], writes=[b_rows])
                m.op(m.dv, lambda: m.nc.vector.tensor_copy(self.rn16b[:], r1),
                     reads=[b_rows], writes=[self.b_rn16b])
                scl = self.pcol[:, 32 + which:33 + which]
                for hf in range(2):
                    def bmms(hf=hf):
                        last = None
                        for c4 in range(4):
                            last = m.nc.tensor.matmul(
                                self.psA[:, c4 * 512:c4 * 512 + 512],
                                self.c_sel16[hf * 4 + c4][:], self.rn16b[:],
                                start=True, stop=True)
                        return last
                    m.wr(m.pe, self.bA)
                    m.op(m.pe, bmms, reads=[self.b_rn16b], writes=list(self.bA))
                    pv3 = self.psA.ap().rearrange("p (c t) -> p c t", c=4)
                    m.op(m.dv, lambda hf=hf, pv3=pv3, scl=scl: m.nc.vector.scalar_tensor_tensor(
                        qn[:, hf * 4:hf * 4 + 4, :], qn[:, hf * 4:hf * 4 + 4, :], scl, pv3,
                        ALU.mult, ALU.mult),
                         reads=list(self.bA) + [self.b_pcol, b_qn], writes=[b_qn])
                stab = tabs[:, 1 + 2 * which, t0:t0 + 512]
                ctab = tabs[:, 0 + 2 * which, t0:t0 + 512]
                stab3 = stab.unsqueeze(1).broadcast_to([128, 4, 512])
                ctab3 = ctab.unsqueeze(1).broadcast_to([128, 4, 512])
                for hf in range(2):
                    pv3 = self.psA.ap().rearrange("p (c t) -> p c t", c=4)
                    def rmms(hf=hf):
                        last = None
                        for c4 in range(4):
                            last = m.nc.tensor.matmul(
                                self.psA[:, c4 * 512:c4 * 512 + 512], self.c_rotH[:],
                                qn[:, hf * 4 + c4, :], start=True, stop=True)
                        return last
                    m.wr(m.pe, self.bA)
                    m.op(m.pe, rmms, reads=[b_qn], writes=[self.bA[0]])
                    for bx in self.bA[1:]:
                        bx.w = self.bA[0].w
                        bx.rs = []
                    m.op(m.dv, lambda pv3=pv3, stab3=stab3: m.nc.vector.tensor_mul(
                        pv3, pv3, stab3),
                         reads=list(self.bA) + [b_tabs], writes=list(self.bA))
                    m.op(m.dv, lambda hf=hf, ctab3=ctab3: m.nc.vector.tensor_mul(
                        qn[:, hf * 4:hf * 4 + 4, :], qn[:, hf * 4:hf * 4 + 4, :], ctab3),
                         reads=[b_qn, b_tabs], writes=[b_qn])
                    if which == 0:
                        dst = qr[:, hf * 4:hf * 4 + 4, par, :]
                        dbuf = b_qrp[par]
                    else:
                        g0 = (2 * sb) % 6
                        dst = kr[:, hf * 4:hf * 4 + 4, g0:g0 + 2, :].rearrange(
                            "p c g t -> p c (g t)")
                        dbuf = b_krp[sb % 3]
                    m.op(m.dv, lambda dst=dst, hf=hf, pv3=pv3: m.nc.vector.tensor_add(
                        dst, qn[:, hf * 4:hf * 4 + 4, :], pv3),
                         reads=[b_qn] + list(self.bA), writes=[dbuf])

        def consume(w):
            par = (w // 2) % 2
            ao = (w % 2) * 256
            g0 = (w - 1) % 6
            g1 = w % 6
            kp0 = (((w - 1) % 6) // 2) % 3
            kp1 = ((w % 6) // 2) % 3
            for c in range(NCH):
                for p in range(2):
                    kst = g0 * 256 + 128 * p
                    qo = (w % 2) * 256 + p * 128
                    for hh in range(2):
                        lo = hh * 64
                        sp = self.psB[:, hh * 512:hh * 512 + 512]
                        bs = self.bB[hh]
                        def smms(c=c, lo=lo, sp=sp, kst=kst, qo=qo):
                            last = None
                            done = 0
                            st = kst
                            while done < 384:
                                if st >= 1536:
                                    st -= 1536
                                n = min(384 - done, 1536 - st)
                                last = m.nc.tensor.matmul(
                                    sp[:, done:done + n],
                                    qr[lo:lo + 64, c, par, qo:qo + 128],
                                    kr.ap().rearrange("p c g t -> p c (g t)")[lo:lo + 64, c, st:st + n],
                                    start=True, stop=True)
                                done += n
                                st += n
                            return last
                        m.op(m.pe, smms, reads=[b_qrp[par], b_krp[kp0], b_krp[kp1]],
                             writes=[bs])
                    s2 = self.psB.ap()[:, 0:1024].rearrange("p (h x) -> p h x", h=2)[:, :, 0:384]
                    mask = self.c_maskL if w > 0 else self.c_maskL0[p]
                    mask3 = mask[:].unsqueeze(1).broadcast_to([128, 2, 384])
                    m.op(m.dv, lambda s2=s2, mask3=mask3: m.nc.vector.tensor_add(
                        s2, s2, mask3),
                         reads=[self.bB[0], self.bB[1]], writes=[self.bB[0], self.bB[1]])
                    m.op(m.dv, lambda s2=s2: m.nc.vector.reduce_max(
                        self.vparam[:, 4:6], s2, axis=AX.X, negate=True),
                         reads=[self.bB[0], self.bB[1]], writes=[self.b_vparam], hz=True)
                    for hh in range(2):
                        lo = hh * 64
                        sp = self.psB[:, hh * 512:hh * 512 + 512]
                        bs = self.bB[hh]
                        m.op(m.sc, lambda sp=sp, hh=hh: m.nc.scalar.activation(
                            P[:], sp[:, 0:384], AF.Exp, bias=self.vparam[:, 4 + hh:5 + hh],
                            accum_out=self.vparam[:, 8 + hh:9 + hh]),
                             reads=[bs, self.b_vparam], writes=[b_P, self.b_vparam])
                        m.op(m.dv, lambda hh=hh: m.nc.vector.reciprocal(
                            self.vparam[:, 10 + hh:11 + hh], self.vparam[:, 8 + hh:9 + hh]),
                             reads=[self.b_vparam], writes=[self.b_vparam], hz=True)
                        m.op(m.dv, lambda hh=hh: m.nc.vector.tensor_scalar_mul(
                            P[:], P[:], self.vparam[:, 10 + hh:11 + hh]),
                             reads=[self.b_vparam], writes=[b_P])
                        def trs():
                            last = None
                            for kk in range(3):
                                last = m.nc.tensor.transpose(
                                    self.psB_bf[:, 2 * 1024 + kk * 128:2 * 1024 + kk * 128 + 128],
                                    P[:, kk * 128:kk * 128 + 128], self.c_id[:])
                            return last
                        m.op(m.pe, trs, reads=[b_P], writes=[self.bB[2]])
                        m.op(m.sc, lambda: m.nc.scalar.copy(
                            PT.ap().rearrange("p k t -> p (k t)"),
                            self.psB_bf[:, 2 * 1024:2 * 1024 + 384]),
                             reads=[self.bB[2]], writes=[b_PT])
                        pvb = self.psB[:, 3 * 512:3 * 512 + 512]
                        def pvm(c=c, lo=lo, hh=hh, p=p, w=w):
                            last = None
                            for j in range(3):
                                vt = (2 * (w - 1) + p + j) % 10
                                last = m.nc.tensor.matmul(
                                    pvb[lo:lo + 64, p * 128:p * 128 + 128],
                                    vr[:, vt, c * 128 + lo:c * 128 + lo + 64],
                                    PT[:, j, :],
                                    start=(j == 0), stop=(j == 2),
                                    tile_position=(0, lo))
                            return last
                        vgs = sorted({((2 * (w - 1) + p + j) % 10) // 2 for j in range(3)})
                        m.op(m.pe, pvm, reads=[b_PT] + [b_vrg[g] for g in vgs],
                             writes=[self.bB[3]])
                m.op(m.sc, lambda c=c, ao=ao: m.nc.scalar.copy(
                    aout[:, c, ao:ao + 256], self.psB[:, 3 * 512:3 * 512 + 512][:, 0:256]),
                     reads=[self.bB[3]], writes=[b_aout])

        def wo_res(sb):
            t0 = 512 * sb
            m.dma(m.qs, "ia_hr", hres[:], [b_hres], hv[:, :, t0:t0 + 512], [],
                  src_key="hin")
            if sb + 1 in hb_ev:
                m.dep(m.qs, hb_ev[sb + 1])
            for oc in range(NCH):
                r = oc % 2
                pb = self.psB[:, r * 512:r * 512 + 512]
                bb = self.bB[r]
                def omms(oc=oc, pb=pb):
                    last = None
                    for e in range(NCH):
                        last = m.nc.tensor.matmul(pb, wo[:, e, oc * 128:oc * 128 + 128],
                                                  aout[:, e, :], start=(e == 0), stop=(e == NCH - 1))
                    return last
                m.op(m.pe, omms, reads=[b_aout, b_wo], writes=[bb])
                sob = so[oc % 2]
                bsob = b_so[oc % 2]
                m.op(m.dv, lambda oc=oc, pb=pb, sob=sob: m.nc.vector.tensor_add(
                    sob[:], pb, hres[:, oc, :]),
                     reads=[bb, b_hres], writes=[bsob])
                m.dma(m.qs, "ia_st", hv[:, oc, t0:t0 + 512], [], sob[:], [bsob],
                      dst_key="hin")

        produce(0)
        for sb in range(4):
            consume(2 * sb)
            if sb + 1 < 4:
                produce(sb + 1)
            consume(2 * sb + 1)
            wo_res(sb)
        for mb, subs in ((b_kr_m, b_krp), (b_vr_m, b_vrg), (b_qr_m, b_qrp)):
            mb.w = None
            mb.rs = [e for sub in subs for e in (sub.rs + ([sub.w] if sub.w else []))]

    def inner_ffn(self, li):
        m = self.m
        cur = [self.fix_cur]
        w2 = self.region("if_w2", [128, 22, 1024], BF16, cur)
        b_w2 = self.B(w2)
        xn = self.region("if_xn", [128, NCH, T], BF16, cur)
        gated = self.region("if_ga", [128, 22, 512], BF16, cur)
        w1r = [self.region(f"if_w1r{i}", [128, NCH, 128], BF16, cur) for i in range(8)]
        hbf = [self.region(f"if_hb{i}", [128, NCH, 272], BF16, cur) for i in range(2)]
        hrf = self.region("if_hr", [128, NCH, 512], F32, cur)
        so = [self.region(f"if_so{i}", [128, 512], F32, cur) for i in range(2)]
        gg = self.region("if_gg", [128, 512], BF16, cur)
        t1 = self.region("if_t1", [128, 256], F32, cur)
        b_xn, b_ga, b_gg = self.B(xn), self.B(gated), self.B(gg)
        b_hrf, b_t1 = self.B(hrf), self.B(t1)
        b_w1r = [self.B(x) for x in w1r]
        b_hbf = [self.B(x) for x in hbf]
        b_so = [self.B(x) for x in so]
        hv = self.hinner.rearrange("c p t -> p c t")
        sq = self.region_alias("if_sq", [128, NCH, 272], BF16, gated)
        b_sq = b_ga

        m.dma(m.qg, "if_w2l", w2[:, 0:21, :], [b_w2],
              self.fw2[li][0:2688, :].rearrange("(j p) o -> p j o", p=128), [])
        m.dma(m.qg, "if_w2r", w2[0:42, 21, :], [b_w2], self.fw2[li][2688:2730, :], [])
        m.dma(m.qs, "pcol", self.pcol[:, 0:8], [self.b_pcol], self.fg[li], [])
        m.dma(m.qs, "pcol", self.pcol[:, 8:16], [self.b_pcol], self.fbeta[li], [])
        m.dma(m.qs, "pcol", self.pcol[:, 16:24], [self.b_pcol], self.fb2[li], [])
        do_bias = not self.bias1_done[li]
        if do_bias:
            m.dma(m.qs, "pcol", self.bias1a[:, li, :], [self.b_bias1], self.fb1a[li], [])
            m.dma(m.qs, "pcol", self.bias1g[:, li, :], [self.b_bias1], self.fb1g[li], [])
            m.op(m.dv, lambda: m.nc.vector.tensor_copy(self.pcolb[:], self.pcol[:, 8:16]),
                 reads=[self.b_pcol], writes=[self.b_pcolb])
            self.bias1_done[li] = True

        def xn_tile(tt):
            par = tt % 2
            t0 = 256 * tt
            if tt == 0:
                m.op(m.dv, lambda: m.nc.vector.memset(hbf[0][:, :, 0:1], 0.0),
                     writes=[b_hbf[0]])
                m.dma(m.qg, f"if_hb{par}", hbf[0][:, :, 1:257], [b_hbf[0]],
                      hv[:, :, 0:256], [], src_key="hin")
            else:
                m.dma(m.qg, f"if_hb{par}", hbf[par][:, :, 0:257], [b_hbf[par]],
                      hv[:, :, t0 - 1:t0 + 256], [], src_key="hin")
            hbs = hbf[par]
            def srhs(c, par=par):
                off = 1 if c < 4 else 0
                return hbf[par][:, c, off:off + 256]
            m.op(m.sc, lambda hbs=hbs: m.nc.scalar.activation(sq[:, :, 0:257], hbs[:, :, 0:257], AF.Square),
                 reads=[b_hbf[par]], writes=[b_sq])
            def sqs(c):
                off = 1 if c < 4 else 0
                return sq[:, c, off:off + 256]
            self.ln_stats(srhs, sqs, [b_hbf[par], b_sq], 256,
                          self.psA[:, 0:512], self.bA[0], self.psA[:, 512:1024], self.bA[1],
                          self.psA[:, 1024:1536], self.bA[2], self.psA[:, 1536:2048], self.bA[3])
            for c in range(NCH):
                m.op(m.dv, lambda c=c, par=par: m.nc.vector.tensor_sub(
                    t1[:], srhs(c), self.psA[:, 1024:1280]),
                     reads=[b_hbf[par], self.bA[2]], writes=[b_t1])
                m.op(m.dv, lambda c=c, t0=t0: m.nc.vector.scalar_tensor_tensor(
                    xn[:, c, t0:t0 + 256], t1[:], self.pcol[:, c:c + 1],
                    self.psA[:, 1536:1792], ALU.mult, ALU.mult),
                     reads=[b_t1, self.bA[3], self.b_pcol], writes=[b_xn])

        rows, b_rows = self.rows, self.b_rows
        xn_tile(0)
        xn_tile(1)
        for tile in range(4):
            tc = slice(tile * 512, tile * 512 + 512)
            if 2 * tile + 2 < 8:
                xn_tile(2 * tile + 2)
            if 2 * tile + 3 < 8:
                xn_tile(2 * tile + 3)
            m.dma(m.qs, "if_hr", hrf[:], [b_hrf], hv[:, :, tc], [], src_key="hin")
            for j in range(22):
                wj = 128 if j < 21 else 42
                sa = w1r[(2 * j) % 8]
                sg = w1r[(2 * j + 1) % 8]
                bsa = b_w1r[(2 * j) % 8]
                bsg = b_w1r[(2 * j + 1) % 8]
                w1v = self.fw1[li].rearrange("(c p) n -> p c n", p=128)
                m.dma(m.qg, f"if_w1_{(2 * j) % 8}", sa[:, :, 0:wj], [bsa],
                      w1v[:, :, 128 * j:128 * j + wj], [])
                m.dma(m.qg, f"if_w1_{(2 * j + 1) % 8}", sg[:, :, 0:wj], [bsg],
                      w1v[:, :, 2730 + 128 * j:2730 + 128 * j + wj], [])
                pa = self.psA[:, (j % 2) * 512:(j % 2) * 512 + 512]
                ba = self.bA[j % 2]
                pg = self.psA[:, (2 + j % 2) * 512:(2 + j % 2) * 512 + 512]
                bgp = self.bA[2 + j % 2]
                def amms(sa=sa, pa=pa, wj=wj, tc=tc):
                    last = None
                    for e in range(NCH):
                        last = m.nc.tensor.matmul(pa[0:wj, :], sa[:, e, 0:wj],
                                                  xn[:, e, tc], start=(e == 0), stop=(e == NCH - 1))
                    return last
                m.op(m.pe, amms, reads=[bsa, b_xn], writes=[ba])
                def gmms(sg=sg, pg=pg, wj=wj, tc=tc):
                    last = None
                    for e in range(NCH):
                        last = m.nc.tensor.matmul(pg[0:wj, :], sg[:, e, 0:wj],
                                                  xn[:, e, tc], start=(e == 0), stop=(e == NCH - 1))
                    return last
                m.op(m.pe, gmms, reads=[bsg, b_xn], writes=[bgp])
                if do_bias and tile == 0:
                    for (sl, bsl, btgt) in ((sa, bsa, self.bias1a), (sg, bsg, self.bias1g)):
                        prow = self.psB[:, 3 * 512:3 * 512 + 512]
                        def bmms(sl=sl, wj=wj, prow=prow):
                            last = None
                            for e in range(NCH):
                                last = m.nc.tensor.matmul(
                                    prow[0:1, 0:wj], self.pcolb[:, e:e + 1],
                                    sl[:, e, 0:wj], start=(e == 0), stop=(e == NCH - 1))
                            return last
                        m.op(m.pe, bmms, reads=[bsl, self.b_pcolb], writes=[self.bB[3]])
                        rrow = rows[0:1, 512:512 + wj]
                        m.op(m.sc, lambda rrow=rrow, prow=prow, wj=wj: m.nc.scalar.copy(
                            rrow, prow[0:1, 0:wj]),
                             reads=[self.bB[3]], writes=[b_rows])
                        ptr = self.psB[:, 2 * 512:2 * 512 + 128]
                        m.op(m.pe, lambda rrow=rrow, ptr=ptr, wj=wj: m.nc.tensor.transpose(
                            ptr[0:wj, 0:1], rrow, self.c_idf32[0:1, 0:1]),
                             reads=[b_rows], writes=[self.bB[2]])
                        m.op(m.dv, lambda btgt=btgt, ptr=ptr, wj=wj, j=j: m.nc.vector.tensor_add(
                            btgt[0:wj, li, j:j + 1], ptr[0:wj, 0:1], btgt[0:wj, li, j:j + 1]),
                             reads=[self.bB[2], self.b_bias1], writes=[self.b_bias1])
                m.op(m.sc, lambda pg=pg, wj=wj, j=j: m.nc.scalar.activation(
                    gg[0:wj, :], pg[0:wj, :], AF.Gelu, bias=self.bias1g[0:wj, li, j:j + 1]),
                     reads=[bgp, self.b_bias1], writes=[b_gg])
                m.op(m.dv, lambda pa=pa, wj=wj, j=j: m.nc.vector.scalar_tensor_tensor(
                    gated[0:wj, j, :], pa[0:wj, :], self.bias1a[0:wj, li, j:j + 1],
                    gg[0:wj, :], ALU.add, ALU.mult),
                     reads=[ba, b_gg, self.b_bias1], writes=[b_ga])
            for oc in range(NCH):
                r = oc % 2
                pb = self.psB[:, r * 512:r * 512 + 512]
                bb = self.bB[r]
                def wmms(oc=oc, pb=pb):
                    last = None
                    for j in range(22):
                        kj = 128 if j < 21 else 42
                        last = m.nc.tensor.matmul(pb, w2[0:kj, j, oc * 128:oc * 128 + 128],
                                                  gated[0:kj, j, :], start=(j == 0), stop=(j == 21))
                    return last
                m.op(m.pe, wmms, reads=[b_ga, b_w2], writes=[bb])
                sob = so[r]
                bsob = b_so[r]
                m.op(m.dv, lambda oc=oc, pb=pb, sob=sob: m.nc.vector.scalar_tensor_tensor(
                    sob[:], pb, self.pcol[:, 16 + oc:17 + oc], hrf[:, oc, :],
                    ALU.add, ALU.add),
                     reads=[bb, b_hrf, self.b_pcol], writes=[bsob])
                m.dma(m.qs, "if_st", hv[:, oc, tc], [], sob[:], [bsob], dst_key="hin")


    def attn_b(self, src, dst, seq, src_key, dst_key):
        m = self.m
        cur = [0]
        xn = self.region("b_xn", [128, NCH, T], BF16, cur)
        K = self.region("b_K", [128, NCH, T], BF16, cur)
        V = self.region("b_V", [128, 16, 1024], BF16, cur)
        w0 = self.region("b_w0", [128, NCH, 1024], BF16, cur)
        w1 = self.region("b_w1", [128, NCH, 1024], BF16, cur)
        tabs = self.region("b_tabs", [128, 2, T], BF16, cur)
        qg = self.region("b_qg", [128, NCH, 512], BF16, cur)
        aout = self.region("b_ao", [128, NCH, 512], BF16, cur)
        rw = self.region_alias("b_rw", [128, NCH, 512], BF16, aout)
        xb = [self.region(f"b_xb{i}", [128, NCH, 512], BF16, cur) for i in range(2)]
        P = self.region_alias("b_P", [128, T], BF16, xb[0])
        xst = self.region("b_xst", [128, NCH, 512], F32, cur)
        sq = self.region_alias("b_sq", [128, NCH, 512], BF16, xst)
        PT = self.region_alias("b_PTg", [128, 16, 512], BF16, xst)
        so = [self.region(f"b_so{i}", [128, 512], F32, cur) for i in range(2)]
        t1 = self.region_alias("b_t1", [128, 512], F32, so[0])
        b_xn, b_K, b_V = self.B(xn), self.B(K), self.B(V)
        b_w0, b_w1, b_tabs = self.B(w0), self.B(w1), self.B(tabs)
        b_qg, b_ao, b_xst = self.B(qg), self.B(aout), self.B(xst)
        b_rw = b_ao
        b_sq = b_xst
        b_xb = [self.B(x) for x in xb]
        b_P = b_xb[0]
        b_PT = b_xst
        b_so = [self.B(x) for x in so]
        b_t1 = b_so[0]

        srcv = src.rearrange("c p t -> p c t")
        dstv = dst.rearrange("c p t -> p c t")
        tag = f"ab{seq}"
        wv = self.wbqkv.rearrange("(c p) o -> p c o", p=128)

        m.dma(m.qg, tag + "w0", w0[:], [b_w0], wv[:, :, 1024:2048], [])
        m.dma(m.qg, tag + "w1", w1[:], [b_w1], wv[:, :, 2048:3072], [])
        m.dma(m.qs, tag + "tab", tabs[:, 0, :], [b_tabs], self.d_bcos.ap(), [])
        m.dma(m.qs, tag + "tab", tabs[:, 1, :], [b_tabs], self.d_bsin.ap(), [])
        m.dma(m.qs, "pcol", self.pcol[:, 24:32], [self.b_pcol], self.bbo.ap(), [])

        def rope(dst_ap, dbufs, raw, b_raw, cols):
            cos3 = tabs[:, 0, cols].unsqueeze(1).broadcast_to([128, 4, 512])
            sin3 = tabs[:, 1, cols].unsqueeze(1).broadcast_to([128, 4, 512])
            for hf in range(2):
                pv = self.psA.ap().rearrange("p (c t) -> p c t", c=4)
                def rmms(hf=hf):
                    last = None
                    for c4 in range(4):
                        last = m.nc.tensor.matmul(
                            self.psA[:, c4 * 512:c4 * 512 + 512], self.c_rotI[:],
                            raw[:, hf * 4 + c4, :], start=True, stop=True)
                    return last
                m.wr(m.pe, self.bA)
                m.op(m.pe, rmms, reads=[b_raw], writes=[self.bA[0]])
                for bx in self.bA[1:]:
                    bx.w = self.bA[0].w
                    bx.rs = []
                m.op(m.dv, lambda pv=pv, sin3=sin3: m.nc.vector.tensor_mul(pv, pv, sin3),
                     reads=list(self.bA) + [b_tabs], writes=list(self.bA))
                m.op(m.dv, lambda hf=hf, cos3=cos3: m.nc.vector.tensor_mul(
                    raw[:, hf * 4:hf * 4 + 4, :], raw[:, hf * 4:hf * 4 + 4, :], cos3),
                     reads=[b_raw, b_tabs], writes=[b_raw])
                m.op(m.dv, lambda hf=hf, pv=pv, dst_ap=dst_ap: m.nc.vector.tensor_add(
                    dst_ap[:, hf * 4:hf * 4 + 4, :], raw[:, hf * 4:hf * 4 + 4, :], pv),
                     reads=[b_raw] + list(self.bA), writes=dbufs)

        for tb in range(4):
            cols = slice(tb * 512, tb * 512 + 512)
            par = tb % 2
            m.dma(m.qg, tag + f"xb{par}", xb[par][:], [b_xb[par]], srcv[:, :, cols], [],
                  src_key=src_key)
            m.op(m.sc, lambda par=par: m.nc.scalar.activation(sq[:], xb[par][:], AF.Square),
                 reads=[b_xb[par]], writes=[b_sq])
            self.ln_stats(lambda c, par=par: xb[par][:, c, :], lambda c: sq[:, c, :],
                          [b_xb[par], b_sq], 512,
                          self.psA[:, 0:512], self.bA[0], self.psA[:, 512:1024], self.bA[1],
                          self.psA[:, 1024:1536], self.bA[2], self.psA[:, 1536:2048], self.bA[3])
            for c in range(NCH):
                m.op(m.dv, lambda c=c, par=par: m.nc.vector.tensor_sub(
                    t1[:], xb[par][:, c, :], self.psA[:, 1024:1536]),
                     reads=[b_xb[par], self.bA[2]], writes=[b_t1])
                m.op(m.dv, lambda c=c, cols=cols: m.nc.vector.tensor_mul(
                    xn[:, c, cols], t1[:], self.psA[:, 1536:2048]),
                     reads=[b_t1, self.bA[3]], writes=[b_xn])
            for oc in range(NCH):
                pk = self.psB[:, (oc % 2) * 512:(oc % 2) * 512 + 512]
                bk = self.bB[oc % 2]
                def kmms(oc=oc, pk=pk, cols=cols):
                    last = None
                    for e in range(NCH):
                        last = m.nc.tensor.matmul(pk, w0[:, e, oc * 128:oc * 128 + 128],
                                                  xn[:, e, cols], start=(e == 0), stop=(e == NCH - 1))
                    return last
                m.op(m.pe, kmms, reads=[b_w0, b_xn], writes=[bk])
                m.op(m.sc, lambda oc=oc, pk=pk: m.nc.scalar.copy(rw[:, oc, :], pk),
                     reads=[bk], writes=[b_rw])
            rope(xn if False else K.ap().rearrange("p c t -> p c t")[:, :, cols]
                 .rearrange("p c t -> p c t") if False else K[:, :, cols],
                 [b_K], rw, b_rw, cols)
            for i in range(4):
                tt = 4 * tb + i
                for oh in range(2):
                    r = (i * 2 + oh) % 2
                    pv = self.psB[:, (2 + r) * 512:(2 + r) * 512 + 512]
                    bv = self.bB[2 + r]
                    def vmms(tt=tt, oh=oh, pv=pv):
                        last = None
                        for e in range(NCH):
                            last = m.nc.tensor.matmul(
                                pv, xn[:, e, tt * 128:tt * 128 + 128],
                                w1[:, e, oh * 512:oh * 512 + 512],
                                start=(e == 0), stop=(e == NCH - 1))
                        return last
                    m.op(m.pe, vmms, reads=[b_w1, b_xn], writes=[bv])
                    m.op(m.sc, lambda tt=tt, oh=oh, pv=pv: m.nc.scalar.copy(
                        V[:, tt, oh * 512:oh * 512 + 512], pv),
                         reads=[bv], writes=[b_V])
        m.dma(m.qg, tag + "w0", w0[:], [b_w0], wv[:, :, 0:1024], [])
        m.dma(m.qg, tag + "w1", w1[:], [b_w1],
              self.wbo.rearrange("(c p) o -> p c o", p=128), [])

        for g in range(4):
            gc = slice(g * 512, g * 512 + 512)
            for oc in range(NCH):
                pq = self.psB[:, (oc % 2) * 512:(oc % 2) * 512 + 512]
                bq = self.bB[oc % 2]
                def qmms(oc=oc, pq=pq, gc=gc):
                    last = None
                    for e in range(NCH):
                        last = m.nc.tensor.matmul(pq, w0[:, e, oc * 128:oc * 128 + 128],
                                                  xn[:, e, gc], start=(e == 0), stop=(e == NCH - 1))
                    return last
                m.op(m.pe, qmms, reads=[b_w0, b_xn], writes=[bq])
                m.op(m.sc, lambda oc=oc, pq=pq: m.nc.scalar.activation(
                    rw[:, oc, :], pq, AF.Copy, scale=1.0 / 8.0),
                     reads=[bq], writes=[b_rw])
            rope(qg[:], [b_qg], rw, b_rw, gc)
            nktg = (g + 1) * 4
            for c in range(NCH):
                for hh in range(2):
                    lo = hh * 64
                    for qt in range(4):
                        qq = slice(qt * 128, qt * 128 + 128)
                        kext = (g * 4 + qt + 1) * 128
                        nkt = kext // 128
                        bS = self.bA[0]
                        def smms(c=c, lo=lo, qq=qq, kext=kext):
                            last = None
                            for kb in range((kext + 511) // 512):
                                n = min(512, kext - kb * 512)
                                last = m.nc.tensor.matmul(
                                    self.psA[:, kb * 512:kb * 512 + n],
                                    qg[lo:lo + 64, c, qq],
                                    K[lo:lo + 64, c, kb * 512:kb * 512 + n],
                                    start=True, stop=True)
                            return last
                        m.wr(m.pe, self.bA)
                        m.op(m.pe, smms, reads=[b_qg, b_K], writes=[bS])
                        for bx in self.bA[1:]:
                            bx.w = bS.w
                            bx.rs = []
                        m.op(m.dv, lambda kext=kext: m.nc.vector.tensor_add(
                            self.psA[:, kext - 128:kext], self.psA[:, kext - 128:kext],
                            self.c_tri[:]),
                             reads=[bS], writes=[bS])
                        m.op(m.dv, lambda kext=kext: m.nc.vector.reduce_max(
                            self.vparam[:, 1:2], self.psA[:, 0:kext], axis=AX.X, negate=True),
                             reads=[bS], writes=[self.b_vparam], hz=True)
                        m.op(m.sc, lambda kext=kext: m.nc.scalar.activation(
                            P[:, 0:kext], self.psA[:, 0:kext], AF.Exp,
                            bias=self.vparam[:, 1:2],
                            accum_out=self.vparam[:, 2:3]),
                             reads=[bS, self.b_vparam], writes=[b_P, self.b_vparam])
                        m.op(m.dv, lambda: m.nc.vector.reciprocal(
                            self.vparam[:, 3:4], self.vparam[:, 2:3]),
                             reads=[self.b_vparam], writes=[self.b_vparam], hz=True)
                        m.op(m.dv, lambda kext=kext: m.nc.vector.tensor_scalar_mul(
                            P[:, 0:kext], P[:, 0:kext], self.vparam[:, 3:4]),
                             reads=[self.b_vparam], writes=[b_P])
                        if kext < nktg * 128:
                            m.op(m.dv, lambda kext=kext: m.nc.vector.memset(
                                P[:, kext:nktg * 128], 0.0),
                                 reads=[], writes=[b_P])
                        for half in range((nktg + 7) // 8):
                            pb0 = (2 + half) * 1024
                            bb = self.bB[2 + half]
                            n8 = min(8, nktg - half * 8)
                            def trs(pb0=pb0, half=half, n8=n8):
                                last = None
                                for k in range(n8):
                                    kt = half * 8 + k
                                    last = m.nc.tensor.transpose(
                                        self.psB_bf[:, pb0 + k * 128:pb0 + k * 128 + 128],
                                        P[:, kt * 128:kt * 128 + 128], self.c_id[:])
                                return last
                            m.op(m.pe, trs, reads=[b_P], writes=[bb])
                            m.op(m.sc, lambda pb0=pb0, half=half, n8=n8, qt=qt: m.nc.scalar.copy(
                                PT[:, half * 8:half * 8 + n8, qt * 128:qt * 128 + 128],
                                self.psB_bf[:, pb0:pb0 + n8 * 128]
                                .rearrange("p (k q) -> p k q", k=n8)),
                                 reads=[bb], writes=[b_PT])
                    pvb = self.psB[:, (c % 2) * 512:(c % 2) * 512 + 512]
                    bpv = self.bB[c % 2]
                    def pvm(c=c, lo=lo, hh=hh, pvb=pvb, nktg=nktg):
                        last = None
                        for kt in range(nktg):
                            last = m.nc.tensor.matmul(
                                pvb[lo:lo + 64, 0:512],
                                V[:, kt, c * 128 + lo:c * 128 + lo + 64],
                                PT[:, kt, :],
                                start=(kt == 0), stop=(kt == nktg - 1),
                                tile_position=(0, lo))
                        return last
                    m.op(m.pe, pvm, reads=[b_PT, b_V], writes=[bpv])
                m.op(m.sc, lambda c=c, pvb=pvb: m.nc.scalar.copy(
                    aout[:, c, :], pvb[:]),
                     reads=[bpv], writes=[b_ao])
            m.dma(m.qs, tag + "xst", xst[:], [b_xst], srcv[:, :, gc], [], src_key=src_key)
            for oc in range(NCH):
                r = 2 + oc % 2
                pb = self.psA[:, (oc % 2) * 512:(oc % 2) * 512 + 512]
                bb = self.bA[oc % 2]
                def omms(oc=oc, pb=pb):
                    last = None
                    for e in range(NCH):
                        last = m.nc.tensor.matmul(pb, w1[:, e, oc * 128:oc * 128 + 128],
                                                  aout[:, e, :], start=(e == 0), stop=(e == NCH - 1))
                    return last
                m.op(m.pe, omms, reads=[b_ao, b_w1], writes=[bb])
                sob = so[oc % 2]
                bsob = b_so[oc % 2]
                m.op(m.dv, lambda oc=oc, pb=pb, sob=sob: m.nc.vector.scalar_tensor_tensor(
                    sob[:], pb, self.pcol[:, 24 + oc:25 + oc], xst[:, oc, :],
                    ALU.add, ALU.add),
                     reads=[bb, b_xst, self.b_pcol], writes=[bsob])
                m.dma(m.qs, tag + "st", dstv[:, oc, gc], [], sob[:], [bsob],
                      dst_key=dst_key)

    def build(self):
        m = self.m
        self.load_consts()
        subs = []
        for s in range(SEQ):
            self.cur_seq = s
            if N_SUB == 1:
                self.attn_a(self.x_in[s], self.out[s], s, None, "out")
            else:
                self.attn_a(self.x_in[s], self.hdram[s], s, None, f"hd{s}")
            if N_SUB >= 2:
                dst = self.out[s] if N_SUB == 2 else self.hdram[s]
                dk = "out" if N_SUB == 2 else f"hd{s}"
                self.ffn_block(self.hdram[s], dst, s, f"hd{s}", dk)
            if N_SUB >= 3:
                dst = self.out[s] if N_SUB == 3 else self.hdram[s]
                dk = "out" if N_SUB == 3 else f"hd{s}"
                self.attn_b(self.hdram[s], dst, s, f"hd{s}", dk)
            if N_SUB >= 4:
                self.ffn_block(self.hdram[s], self.out[s], s, f"hd{s}", "out")
        for name, ds in m._dsems.items():
            m.qs.h.wait_ge(ds.sem, ds.n)
        return m.nc


def _host_inputs(inputs):
    g = {}
    g["waq"] = np.ascontiguousarray(inputs["L0_attn_a_Wq"].T)
    g["wak"] = np.ascontiguousarray(inputs["L0_attn_a_Wk"].T)
    g["wav"] = np.ascontiguousarray(inputs["L0_attn_a_Wv"].T)
    g["wbqkv"] = np.ascontiguousarray(inputs["L0_attn_b_Wqkv"])
    g["wbo"] = np.ascontiguousarray(inputs["L0_attn_b_Wo"])
    g["bbo"] = np.ascontiguousarray(inputs["L0_attn_b_bo"].reshape(NCH, 128).T)
    g["fqkv"] = np.ascontiguousarray(inputs["L0_ffn_Wqkv"])
    g["fwo"] = np.ascontiguousarray(inputs["L0_ffn_Wo"])
    g["fw1"] = np.ascontiguousarray(inputs["L0_ffn_W1"])
    g["fw2"] = np.ascontiguousarray(inputs["L0_ffn_W2"])
    b1 = inputs["L0_ffn_b1"]
    fb1a = np.zeros((6, 128, 22), np.float32)
    fb1g = np.zeros((6, 128, 22), np.float32)
    for j in range(22):
        w = min(128, 2730 - 128 * j)
        fb1a[:, 0:w, j] = b1[:, 128 * j:128 * j + w]
        fb1g[:, 0:w, j] = b1[:, 2730 + 128 * j:2730 + 128 * j + w]
    g["fb1a"] = fb1a
    g["fb1g"] = fb1g
    g["fb2"] = np.ascontiguousarray(inputs["L0_ffn_b2"].reshape(6, NCH, 128).transpose(0, 2, 1))
    g["fg"] = np.ascontiguousarray(inputs["L0_ffn_ln_g"].reshape(6, NCH, 128).transpose(0, 2, 1))
    g["fbeta"] = np.ascontiguousarray(inputs["L0_ffn_ln_b"].reshape(6, NCH, 128).transpose(0, 2, 1))
    g["fqs"] = np.ascontiguousarray(np.tile(inputs["L0_ffn_q_scale"], (1, 2)).reshape(6, 128, 1))
    g["fks"] = np.ascontiguousarray(np.tile(inputs["L0_ffn_k_scale"], (1, 2)).reshape(6, 128, 1))
    return {k: v.astype(np.float32, copy=False) for k, v in g.items()}


def kernel(**inputs):
    x = np.asarray(inputs["L0_attn_a_x"], dtype=np.float32)
    B = x.shape[0]
    per = B // 8
    g = _host_inputs(inputs)
    model = Model()
    nc = model.build()
    in_maps = []
    for c in range(8):
        xs = x[c * per:(c + 1) * per]
        xfm = np.ascontiguousarray(xs.transpose(0, 2, 1)).reshape(SEQ, NCH, 128, T)
        im = {"x_fm": xfm}
        im.update(g)
        in_maps.append(im)
    res = run_bass_kernel_spmd(nc, in_maps, core_ids=list(range(8)))
    outs = []
    for c in range(8):
        o = np.asarray(res.results[c]["out_fm"], dtype=np.float32)
        o = o.reshape(SEQ, D, T).transpose(0, 2, 1)
        outs.append(o)
    return np.ascontiguousarray(np.concatenate(outs, axis=0))
